# version 01
# baseline (speedup 1.0000x reference)
import numpy as np
from contextlib import ExitStack
import concourse.bass as bass
import concourse.mybir as mybir
from concourse.bass_utils import run_bass_kernel_spmd

F32 = mybir.dt.float32
BF16 = mybir.dt.bfloat16
FP16 = mybir.dt.float16
AF = mybir.ActivationFunctionType
ALU = mybir.AluOpType
AX = mybir.AxisListType

T = 2048
E = 1024
DH = 64
NH = 16
NLOC = 4
NHASH = 4
S = 8192
CS = 64
NCHUNK = 128
NTILE = 16
FC = 8
MVAL = -3.4e38
TSV = -50000.0
LNE = 1e-5
EPSN = 1e-20

ENGS = ("pe", "dve", "act", "sync", "gp")
SEM_OF = {"pe": "pe", "dve": "dve", "act": "act", "sync": "sd", "gp": "gd"}


class Sched:

    def __init__(self):
        self.ops = {e: [] for e in ENGS}
        self.c = {s: 0 for s in ("pe", "dve", "act", "sd", "gd", "od")}

    def emit(self, eng, fn, waits=(), inc=None, inc_amt=1):
        w = []
        for item in waits:
            if isinstance(item, str):
                w.append((item, self.c[item]))
            else:
                w.append(item)
        tgt = None
        if inc is not None:
            self.c[inc] += inc_amt
            tgt = (inc, self.c[inc])
        self.ops[eng].append((w, fn, tgt))
        return tgt

    def now(self, sem):
        return self.c[sem]


def build_nc(stages=99):
    nc = bass.Bass("TRN2", target_bir_lowering=False)
    dt = nc.dram_tensor

    xin = dt("xin", [2, T, E], F32, kind="ExternalInput").ap()
    w_qk = dt("w_qk", [8, 128, 1024], F32, kind="ExternalInput").ap()
    w_v = dt("w_v", [8, 128, 1024], F32, kind="ExternalInput").ap()
    w_out = dt("w_out", [8, 128, 1024], F32, kind="ExternalInput").ap()
    w_ff1 = dt("w_ff1", [32, 128, 1024], F32, kind="ExternalInput").ap()
    w_ff2 = dt("w_ff2", [4096, 1024], F32, kind="ExternalInput").ap()
    w_r = dt("w_r", [8, 128, 1024], F32, kind="ExternalInput").ap()
    w_k = dt("w_k", [8, 128, 1024], F32, kind="ExternalInput").ap()
    w_v2 = dt("w_v2", [8, 128, 1024], F32, kind="ExternalInput").ap()
    rotc = dt("rotc", [DH, 128], F32, kind="ExternalInput").ap()
    pcols = dt("pcols", [128, 64], F32, kind="ExternalInput").ap()
    bff1c = dt("bff1c", [128, 32], F32, kind="ExternalInput").ap()
    idn_bf = dt("idn_bf", [128, 128], BF16, kind="ExternalInput").ap()
    idn_f16 = dt("idn_f16", [128, 128], FP16, kind="ExternalInput").ap()
    supper_bf = dt("supper_bf", [128, 128], BF16, kind="ExternalInput").ap()
    slt_a = dt("slt_a", [128, 160], F32, kind="ExternalInput").ap()
    slt_b = dt("slt_b", [32, 160], F32, kind="ExternalInput").ap()
    iota64r = dt("iota64r", [128, 64], F32, kind="ExternalInput").ap()
    iota128r = dt("iota128r", [128, 128], F32, kind="ExternalInput").ap()
    iota2048r = dt("iota2048r", [128, 2048], FP16, kind="ExternalInput").ap()
    iotacol = dt("iotacol", [128, 1], F32, kind="ExternalInput").ap()
    thresh64 = dt("thresh64", [128, 64], F32, kind="ExternalInput").ap()
    ones_f32 = dt("ones_f32", [128, 128], F32, kind="ExternalInput").ap()
    ones_bf = dt("ones_bf", [128, 128], BF16, kind="ExternalInput").ap()
    ones_f16 = dt("ones_f16", [128, 128], FP16, kind="ExternalInput").ap()
    mask_w = dt("mask_w", [128, 256], F32, kind="ExternalInput").ap()
    mask_w0 = dt("mask_w0", [128, 256], F32, kind="ExternalInput").ap()
    constc = dt("constc", [128, 4], F32, kind="ExternalInput").ap()
    xout = dt("xout", [2, T, E], F32, kind="ExternalOutput").ap()

    sch = Sched()

    with ExitStack() as stack:
        en = stack.enter_context

        def sb(name, shp, d=BF16):
            return en(nc.sbuf_tensor(name, shp, d))

        Z0 = sb("Z0", [128, NTILE * E])
        AO = sb("AO", [128, NTILE * E])
        A1 = sb("A1", [128, NTILE * E])
        A2 = sb("A2", [128, NTILE * E])
        A3 = sb("A3", [128, NTILE * E])
        WS = [sb("WS%d" % i, [128, 1024]) for i in range(4)]
        QKG = sb("QKG", [128, T])
        VG = sb("VG", [128, T])
        QT = sb("QT", [128, 1024])
        VT = sb("VT", [128, 17 * 64])
        KH = sb("KH", [64, 2176])
        QH = sb("QH", [64, T])
        ATN = sb("ATN", [128, 1024])
        PT = sb("PT", [128, 256])
        s_idn = sb("s_idn", [128, 128])
        s_idn16 = sb("s_idn16", [128, 128], FP16)
        s_sup = sb("s_sup", [128, 128])
        s_slta = sb("s_slta", [128, 160], F32)
        s_sltb = sb("s_sltb", [32, 160], F32)
        s_i64 = sb("s_i64", [128, 64], F32)
        s_i128 = sb("s_i128", [128, 128], F32)
        s_i2048 = sb("s_i2048", [128, 2048], FP16)
        s_icol = sb("s_icol", [128, 1], F32)
        s_th64 = sb("s_th64", [128, 64], F32)
        s_ones32 = sb("s_ones32", [128, 128], F32)
        s_onesbf = sb("s_onesbf", [128, 128])
        s_ones16 = sb("s_ones16", [128, 128], FP16)
        s_maskw = sb("s_maskw", [128, 256], F32)
        s_maskw0 = sb("s_maskw0", [128, 256], F32)
        s_rot = sb("s_rot", [DH, 128])
        s_pc = sb("s_pc", [128, 64], F32)
        s_bff1 = sb("s_bff1", [128, 32], F32)
        s_cc = sb("s_cc", [128, 4], F32)
        st = sb("st", [128, 64], F32)
        t32a = sb("t32a", [128, 1024], F32)
        t32b = sb("t32b", [128, 512], F32)
        pb = [en(nc.psum_tensor("pb%d" % i, [128, 512], F32)) for i in range(8)]
        pbank = lambda i: pb[i].ap()
        pbank16 = lambda i: pb[i].ap().bitcast(BF16)

        sems = {}
        for s in ("pe", "dve", "act", "sd", "gd", "od"):
            sems[s] = en(nc.semaphore("sem_" + s))
        block = en(nc.Block())

        def W(eng, wlist):
            for sname, val in wlist:
                if val > 0:
                    eng.wait_ge(sems[sname], val)

        def dma(queue, out, in_, waits=()):
            semn = "sd" if queue == "sync" else "gd"
            def fn(e, w, out=out, in_=in_, semn=semn):
                W(e, w)
                e.dma_start(out=out, in_=in_).then_inc(sems[semn], 16)
            return sch.emit(queue, fn, waits=waits, inc=semn, inc_amt=16)

        class _Px:
            def __init__(self, e):
                object.__setattr__(self, "_e", e)
                object.__setattr__(self, "last", None)
            def __getattr__(self, name):
                a = getattr(object.__getattribute__(self, "_e"), name)
                if callable(a):
                    def call(*args, **kw):
                        r = a(*args, **kw)
                        if hasattr(r, "then_inc"):
                            object.__setattr__(self, "last", r)
                        return r
                    return call
                return a

        def comp(eng, fn_body, waits=(), inc=True):
            semn = SEM_OF[eng]
            def fn(e, w, fn_body=fn_body, semn=semn, inc=inc):
                W(e, w)
                px = _Px(e)
                fn_body(px)
                if inc:
                    px.last.then_inc(sems[semn], 1)
            return sch.emit(eng, fn, waits=waits, inc=(semn if inc else None))

        for dst, src in [
            (s_idn[:], idn_bf), (s_idn16[:], idn_f16), (s_sup[:], supper_bf),
            (s_slta[:], slt_a), (s_sltb[:], slt_b), (s_i64[:], iota64r),
            (s_i128[:], iota128r), (s_i2048[:], iota2048r), (s_icol[:], iotacol),
            (s_th64[:], thresh64), (s_ones32[:], ones_f32), (s_onesbf[:], ones_bf),
            (s_ones16[:], ones_f16), (s_maskw[:], mask_w), (s_maskw0[:], mask_w0),
            (s_pc[:], pcols), (s_bff1[:], bff1c), (s_cc[:], constc),
        ]:
            dma("sync", dst, src)
        dma("gp", s_rot[:], rotc)
        statics_sd = sch.now("sd")
        statics_gd = sch.now("gd")

        def ln_tm(arena_src, arena_dst, waits, gcol=None, bcol=None):
            def mk_sq(a):
                for ti in range(NTILE):
                    a.activation(out=t32a[:], in_=arena_src[:, ti * E:(ti + 1) * E],
                                 func=AF.Square, accum_out=st[:, 16 + ti:17 + ti])
            sqt = comp("act", mk_sq, waits=waits)

            def mk_s1(v):
                for ti in range(NTILE):
                    v.tensor_reduce(out=st[:, ti:ti + 1],
                                    in_=arena_src[:, ti * E:(ti + 1) * E],
                                    axis=AX.X, op=ALU.add)
                v.drain()
                v.tensor_scalar(out=st[:, 0:16], in0=st[:, 0:16],
                                scalar1=1.0 / E, scalar2=None, op0=ALU.mult)
                v.tensor_scalar(out=st[:, 16:32], in0=st[:, 16:32],
                                scalar1=1.0 / E, scalar2=None, op0=ALU.mult)
                v.drain()
                v.tensor_tensor(out=st[:, 32:48], in0=st[:, 0:16],
                                in1=st[:, 0:16], op=ALU.mult)
                v.drain()
                v.tensor_tensor(out=st[:, 48:64], in0=st[:, 16:32],
                                in1=st[:, 32:48], op=ALU.subtract)
            s1t = comp("dve", mk_s1, waits=[sqt])

            def mk_sqrt(a):
                a.activation(out=st[:, 48:64], in_=st[:, 48:64], func=AF.Sqrt,
                             bias=s_cc[:, 0:1], scale=1.0)
            sqr = comp("act", mk_sqrt, waits=[s1t])

            def mk_apply(v):
                v.reciprocal(out=st[:, 48:64], in_=st[:, 48:64])
                v.drain()
                for ti in range(NTILE):
                    v.tensor_scalar(
                        out=arena_dst[:, ti * E:(ti + 1) * E],
                        in0=arena_src[:, ti * E:(ti + 1) * E],
                        scalar1=st[:, ti:ti + 1], scalar2=st[:, 48 + ti:49 + ti],
                        op0=ALU.subtract, op1=ALU.mult)
            return comp("dve", mk_apply, waits=[sqr])

        def tm_to_fm(src, dst, waits, gcol=None, bcol=None, banks=(0, 1)):
            evs = [None, None]
            first = True
            for fc in range(FC):
                for g in range(4):
                    bk = banks[(fc * 4 + g) % 2]
                    w = list(waits) if first else []
                    if evs[(fc * 4 + g) % 2] is not None:
                        w.append(evs[(fc * 4 + g) % 2])

                    def mk_tr(t, fc=fc, g=g, bk=bk, src=src):
                        for j in range(4):
                            ti = g * 4 + j
                            t.transpose(pbank16(bk)[:, j * 128:(j + 1) * 128],
                                        src[:, ti * E + fc * 128: ti * E + fc * 128 + 128],
                                        s_idn[:])
                    petgt = comp("pe", mk_tr, waits=w)

                    def mk_ev(v, fc=fc, g=g, bk=bk, dst=dst, gcol=gcol, bcol=bcol):
                        dap = dst[:, fc * T + g * 512: fc * T + (g + 1) * 512]
                        if gcol is None:
                            v.tensor_copy(out=dap, in_=pbank16(bk)[:, 0:512])
                        else:
                            v.tensor_scalar(out=dap, in0=pbank16(bk)[:, 0:512],
                                            scalar1=gcol(fc), scalar2=bcol(fc),
                                            op0=ALU.mult, op1=ALU.add)
                    evs[(fc * 4 + g) % 2] = comp("dve", mk_ev, waits=[petgt])
                    first = False
            return evs

        def fm_to_tm(src, dst, waits, banks=(0, 1)):
            evs = [None, None]
            first = True
            n = 0
            for ti in range(NTILE):
                for fg in range(2):
                    bk = banks[n % 2]
                    w = list(waits) if first else []
                    if evs[n % 2] is not None:
                        w.append(evs[n % 2])

                    def mk_tr(t, ti=ti, fg=fg, bk=bk, src=src):
                        for j in range(4):
                            fc = fg * 4 + j
                            t.transpose(pbank16(bk)[:, j * 128:(j + 1) * 128],
                                        src[:, fc * T + ti * 128: fc * T + (ti + 1) * 128],
                                        s_idn[:])
                    petgt = comp("pe", mk_tr, waits=w)

                    def mk_ev(v, ti=ti, fg=fg, bk=bk, dst=dst):
                        dap = dst[:, ti * E + fg * 512: ti * E + (fg + 1) * 512]
                        v.tensor_copy(out=dap, in_=pbank16(bk)[:, 0:512])
                    evs[n % 2] = comp("dve", mk_ev, waits=[petgt])
                    first = False
                    n += 1
            return evs

        def proj_fm(w_ap, n_oc, n_ic, src_fm, ev_fn, waits, wbufs=(0, 1),
                    pbanks=(2, 3), src_tag=None):
            ldtgts = {}
            ldtgts[0] = dma("gp", WS[wbufs[0]][:, 0:n_ic * 128],
                            w_ap[0], waits=[("pe", sch.now("pe"))])
            evtg = [None, None]
            mmtg = {}
            for oc in range(n_oc):
                if oc + 1 < n_oc:
                    w = []
                    if oc >= 1:
                        w.append(mmtg[(oc - 1, 3)])
                    ldtgts[oc + 1] = dma("gp",
                                         WS[wbufs[(oc + 1) % 2]][:, 0:n_ic * 128],
                                         w_ap[oc + 1], waits=w)
                for tsl in range(4):
                    bk = pbanks[(oc * 4 + tsl) % 2]
                    w = [ldtgts[oc]]
                    if oc == 0 and tsl == 0:
                        w += list(waits)
                    if evtg[(oc * 4 + tsl) % 2] is not None:
                        w.append(evtg[(oc * 4 + tsl) % 2])

                    def mk_mm(t, oc=oc, tsl=tsl, bk=bk):
                        wb = WS[wbufs[oc % 2]]
                        for ic in range(n_ic):
                            t.matmul(pbank(bk)[:],
                                     wb[:, ic * 128:(ic + 1) * 128],
                                     src_fm[:, ic * T + tsl * 512: ic * T + (tsl + 1) * 512],
                                     start=(ic == 0), stop=(ic == n_ic - 1))
                    mmtg[(oc, tsl)] = comp("pe", mk_mm, waits=w)

                    def mk_ev(v, oc=oc, tsl=tsl, bk=bk):
                        ev_fn(v, pbank(bk), oc, tsl)
                    evtg[(oc * 4 + tsl) % 2] = comp(
                        "dve", mk_ev, waits=[mmtg[(oc, tsl)]])
            return evtg

        def local_head(b, g, sub, qkv_done):
            r = 64 * sub
            fc = g
            idn64 = s_idn[r:r + 64, r:r + 64]
            qh_ld = dma("sync", QH[:], QKG[r:r + 64, :],
                        waits=list(qkv_done) + [("pe", sch.now("pe"))])
            evq = {}
            for grp in range(4):
                w = list(qkv_done) if grp == 0 else []
                if grp >= 2:
                    w.append(evq[grp - 2])

                def mk_tr4(t, grp=grp, r=r):
                    bk = pbank16(4 + grp % 2)
                    for j in range(4):
                        ti = grp * 4 + j
                        t.transpose(bk[:, j * 64:(j + 1) * 64],
                                    QKG[r:r + 64, ti * 128:(ti + 1) * 128],
                                    idn64)
                ptg = comp("pe", mk_tr4, waits=w)

                def mk_evq(v, grp=grp):
                    v.tensor_copy(out=QT[:, grp * 256:(grp + 1) * 256],
                                  in_=pbank16(4 + grp % 2)[:, 0:256])
                evq[grp] = comp("dve", mk_evq, waits=[ptg])
            def mk_nrm(a):
                for ti in range(NTILE):
                    a.activation(out=t32a[:, 0:64], in_=QT[:, ti * 64:(ti + 1) * 64],
                                 func=AF.Square, accum_out=st[:, ti:ti + 1])
            nrm = comp("act", mk_nrm, waits=[evq[2], evq[3]])

            def mk_sqrt(a):
                a.activation(out=st[:, 16:32], in_=st[:, 0:16], func=AF.Sqrt,
                             bias=s_cc[:, 1:2], scale=1.0)
            sq = comp("act", mk_sqrt, waits=[nrm])

            def mk_inv(v):
                v.reciprocal(out=st[:, 16:32], in_=st[:, 16:32])
                v.drain()
                for ti in range(NTILE):
                    v.tensor_scalar(out=QT[:, ti * 64:(ti + 1) * 64],
                                    in0=QT[:, ti * 64:(ti + 1) * 64],
                                    scalar1=st[:, 16 + ti:17 + ti], scalar2=None,
                                    op0=ALU.mult)
            inv = comp("dve", mk_inv, waits=[sq])

            def mk_z(v):
                v.memset(KH[0:64, 0:128], 0.0)
            khz = comp("dve", mk_z, waits=())
            evk = {}
            for grp in range(4):
                w = [inv] if grp == 0 else []
                if grp >= 2:
                    w.append(evk[grp - 2])

                def mk_tr4(t, grp=grp):
                    bk = pbank16(4 + grp % 2)
                    for j in range(4):
                        ti = grp * 4 + j
                        t.transpose(bk[0:64, j * 128:(j + 1) * 128],
                                    QT[:, ti * 64:(ti + 1) * 64],
                                    s_idn[:])
                ptg = comp("pe", mk_tr4, waits=w)

                def mk_evk(v, grp=grp):
                    v.tensor_copy(out=KH[:, 128 + grp * 512:128 + (grp + 1) * 512],
                                  in_=pbank16(4 + grp % 2)[0:64, 0:512])
                evk[grp] = comp("dve", mk_evk, waits=[ptg])

            def mk_vz(v):
                v.memset(VT[:, 0:64], 0.0)
            vtz = comp("dve", mk_vz, waits=())
            evv = {}
            for grp in range(4):
                w = list(qkv_done) if grp == 0 else []
                if grp >= 2:
                    w.append(evv[grp - 2])

                def mk_tr4(t, grp=grp, r=r):
                    bk = pbank16(6 + grp % 2)
                    for j in range(4):
                        ti = grp * 4 + j
                        t.transpose(bk[:, j * 64:(j + 1) * 64],
                                    VG[r:r + 64, ti * 128:(ti + 1) * 128],
                                    idn64)
                ptg = comp("pe", mk_tr4, waits=w)

                def mk_evv(v, grp=grp):
                    v.tensor_copy(out=VT[:, 64 + grp * 256:64 + (grp + 1) * 256],
                                  in_=pbank16(6 + grp % 2)[:, 0:256])
                evv[grp] = comp("dve", mk_evv, waits=[ptg])

            lastdots = {}
            lastout = None
            for wg in range(8):
                bk = wg % 2
                w = [evk[3], qh_ld, ("dve", sch.now("dve"))] if wg == 0 else []
                if wg >= 2:
                    w.append(lastdots[wg % 2])

                def mk_dots(t, wg=wg, bk=bk, r=r):
                    for j in range(2):
                        wi = wg * 2 + j
                        t.matmul(pbank(bk)[:, j * 256:(j + 1) * 256],
                                 QH[:, wi * 128:(wi + 1) * 128],
                                 KH[:, wi * 128:wi * 128 + 256],
                                 start=True, stop=True)
                dtg = comp("pe", mk_dots, waits=w)

                def mk_sm(v, wg=wg, bk=bk):
                    for j in range(2):
                        wi = wg * 2 + j
                        msk = s_maskw0 if wi == 0 else s_maskw
                        v.scalar_tensor_tensor(
                            out=t32a[:, j * 256:(j + 1) * 256],
                            in0=pbank(bk)[:, j * 256:(j + 1) * 256],
                            scalar=0.125, in1=msk[:],
                            op0=ALU.mult, op1=ALU.add)
                    v.drain()
                    v.tensor_reduce(out=st[:, 32:34],
                                    in_=t32a[:, 0:512].rearrange("p (w k) -> p w k", w=2),
                                    axis=AX.X, op=ALU.max)
                    v.drain()
                    v.tensor_tensor(
                        out=t32a[:, 0:512].rearrange("p (w k) -> p w k", w=2),
                        in0=t32a[:, 0:512].rearrange("p (w k) -> p w k", w=2),
                        in1=st[:, 32:34].unsqueeze(2).to_broadcast((128, 2, 256)),
                        op=ALU.subtract)
                smt = comp("dve", mk_sm, waits=[dtg])

                def mk_exp(a):
                    a.activation(out=t32b[:, 0:512], in_=t32a[:, 0:512], func=AF.Exp)
                ext = comp("act", mk_exp, waits=[smt])

                def mk_nrm2(v, wg=wg):
                    v.tensor_reduce(out=st[:, 34:36],
                                    in_=t32b[:, 0:512].rearrange("p (w k) -> p w k", w=2),
                                    axis=AX.X, op=ALU.add)
                    v.drain()
                    v.reciprocal(out=st[:, 34:36], in_=st[:, 34:36])
                    v.drain()
                    aslice = ATN[:, 0:512] if wg % 2 == 0 else ATN[:, 512:1024]
                    v.tensor_tensor(
                        out=aslice.rearrange("p (w k) -> p w k", w=2),
                        in0=t32b[:, 0:512].rearrange("p (w k) -> p w k", w=2),
                        in1=st[:, 34:36].unsqueeze(2).to_broadcast((128, 2, 256)),
                        op=ALU.mult)
                lastdots[wg % 2] = comp("dve", mk_nrm2, waits=[ext])

                for j in range(2):
                    wi = wg * 2 + j
                    aoff = (wg % 2) * 512 + j * 256

                    def mk_ptr(t, aoff=aoff):
                        t.transpose(pbank16(2)[:, 0:128], ATN[:, aoff:aoff + 128], s_idn[:])
                        t.transpose(pbank16(2)[:, 128:256], ATN[:, aoff + 128:aoff + 256], s_idn[:])
                    ptt = comp("pe", mk_ptr, waits=[lastdots[wg % 2]] + ([lastout] if lastout else []))

                    def mk_pte(v):
                        v.tensor_copy(out=PT[:], in_=pbank16(2)[:, 0:256])
                    pte = comp("dve", mk_pte, waits=[ptt])

                    def mk_out(t, wi=wi):
                        t.matmul(pbank(3)[0:64, 0:128], VT[:, wi * 64:(wi + 1) * 64],
                                 PT[:, 0:128], start=True, stop=False)
                        t.matmul(pbank(3)[0:64, 0:128], VT[:, (wi + 1) * 64:(wi + 2) * 64],
                                 PT[:, 128:256], start=False, stop=True)
                    ott = comp("pe", mk_out, waits=[pte])

                    def mk_oev(v, wi=wi, r=r, fc=fc):
                        v.tensor_copy(
                            out=AO[r:r + 64, fc * T + wi * 128: fc * T + (wi + 1) * 128],
                            in_=pbank(3)[0:64, 0:128])
                    lastout = comp("dve", mk_oev, waits=[ott])

        def lsh_head(b, g, sub, qkv_done):
            pass

        for b in range(2):
            if stages <= 0:
                for ti in range(NTILE):
                    w = [("sd", sch.now("sd"))]
                    ltg = dma("gp", t32a[:, 0:E], xin[b, ti * 128:(ti + 1) * 128, :], waits=w)
                    dma("sync", xout[b, ti * 128:(ti + 1) * 128, :], t32a[:, 0:E], waits=[ltg])
                continue
            xld = []
            for ti in range(NTILE):
                w = [("dve", sch.now("dve")), ("sd", sch.now("sd"))] if ti == 0 else []
                xld.append(dma("gp", A1[:, ti * E:(ti + 1) * E],
                               xin[b, ti * 128:(ti + 1) * 128, :], waits=w))
            if stages == 100:
                for ti in range(NTILE):
                    w = [("gd", sch.now("gd"))] if ti == 0 else [("sd", sch.now("sd"))]
                    def mk_c(v, ti=ti):
                        v.tensor_copy(out=t32a[:, 0:E], in_=A1[:, ti * E:(ti + 1) * E])
                    ftg = comp("dve", mk_c, waits=w)
                    dma("sync", xout[b, ti * 128:(ti + 1) * 128, :], t32a[:, 0:E], waits=[ftg])
                continue
            lastz = ln_tm(A1, A2, waits=("gd",))
            if stages > 1:
                tm_to_fm(A2, Z0, waits=[lastz])

            if stages <= 1:
                for ti in range(NTILE):
                    w = [("dve", sch.now("dve")), ("pe", sch.now("pe"))] if ti == 0 else []
                    def mk_c(v, ti=ti):
                        v.tensor_copy(out=t32a[:, 0:E], in_=A2[:, ti * E:(ti + 1) * E])
                    ftg = comp("dve", mk_c, waits=w + ([("sd", sch.now("sd"))] if ti > 0 else []))
                    dma("sync", xout[b, ti * 128:(ti + 1) * 128, :], t32a[:, 0:E], waits=[ftg])
                continue

            def mk_zero(v):
                v.memset(AO[:], 0.0)
            aoz = comp("dve", mk_zero, waits=())

            for g in range(8):
                lq = dma("gp", WS[0][:, :], w_qk[g], waits=[("pe", sch.now("pe"))])
                lv = dma("gp", WS[1][:, :], w_v[g], waits=[("pe", sch.now("pe"))])
                prev_ev = {}
                for which, (wld, dst) in enumerate([(lq, QKG), (lv, VG)]):
                    for tsl in range(4):
                        bk = 2 + (which * 4 + tsl) % 2
                        w = [wld]
                        if (which * 4 + tsl) >= 2:
                            w.append(prev_ev[(which * 4 + tsl) % 2])
                        if g == 0 and which == 0 and tsl == 0:
                            w.append(("dve", sch.now("dve")))

                        def mk_mm(t, which=which, tsl=tsl, bk=bk):
                            wb = WS[which]
                            for ic in range(8):
                                t.matmul(pbank(bk)[:],
                                         wb[:, ic * 128:(ic + 1) * 128],
                                         Z0[:, ic * T + tsl * 512: ic * T + (tsl + 1) * 512],
                                         start=(ic == 0), stop=(ic == 7))
                        mtg = comp("pe", mk_mm, waits=w)

                        def mk_ev(v, tsl=tsl, bk=bk, dst=dst):
                            v.tensor_copy(out=dst[:, tsl * 512:(tsl + 1) * 512],
                                          in_=pbank(bk)[:])
                        prev_ev[(which * 4 + tsl) % 2] = comp("dve", mk_ev, waits=[mtg])
                qkv_done = [prev_ev[0], prev_ev[1]]

                for sub in range(2):
                    hd = g * 2 + sub
                    if hd < NLOC:
                        local_head(b, g, sub, qkv_done)
                    else:
                        lsh_head(b, g, sub, qkv_done)

            def sa_ev(v, bank, oc, tsl):
                v.scalar_tensor_tensor(
                    out=A1[:, oc * T + tsl * 512: oc * T + (tsl + 1) * 512],
                    in0=bank[:], scalar=s_pc[:, oc:oc + 1],
                    in1=Z0[:, oc * T + tsl * 512: oc * T + (tsl + 1) * 512],
                    op0=ALU.add, op1=ALU.add)
            proj_fm(w_out, 8, 8, AO, sa_ev, waits=[aoz, ("dve", sch.now("dve"))])

            if stages <= 2:
                for ti in range(NTILE):
                    w = [("dve", sch.now("dve"))] if ti == 0 else [("sd", sch.now("sd"))]
                    def mk_c(v, ti=ti):
                        for fc in range(FC):
                            v.tensor_copy(out=t32a[:, fc * 128:(fc + 1) * 128],
                                          in_=A1[:, fc * T + ti * 128: fc * T + (ti + 1) * 128])
                    ftg = comp("dve", mk_c, waits=w)
                    dma("sync", xout[b, ti * 128:(ti + 1) * 128, :], t32a[:, 0:E], waits=[ftg])
                continue

            fm_to_tm(A1, A2, waits=[("dve", sch.now("dve"))])
            l1 = ln_tm(A2, A2, waits=[("dve", sch.now("dve"))])
            tm_to_fm(A2, A1, waits=[l1],
                     gcol=lambda fc: s_pc[:, 8 + fc:9 + fc],
                     bcol=lambda fc: s_pc[:, 16 + fc:17 + fc])
            y1_done = ("dve", sch.now("dve"))
            if stages <= 3:
                for ti in range(NTILE):
                    w = [y1_done] if ti == 0 else [("sd", sch.now("sd"))]
                    def mk_c(v, ti=ti):
                        for fc in range(FC):
                            v.tensor_copy(out=t32a[:, fc * 128:(fc + 1) * 128],
                                          in_=A1[:, fc * T + ti * 128: fc * T + (ti + 1) * 128])
                    ftg = comp("dve", mk_c, waits=w)
                    dma("sync", xout[b, ti * 128:(ti + 1) * 128, :], t32a[:, 0:E], waits=[ftg])
                continue

            harena = {0: A3, 1: AO}
            prev_ff2_ev = None
            for half in range(2):
                ld = {}
                w0 = [("pe", sch.now("pe"))]
                if prev_ff2_ev is not None:
                    w0.append(prev_ff2_ev)
                ld[0] = dma("gp", WS[0][:, :], w_ff1[0], waits=w0)
                evh = [None, None]
                mmh = {}
                for oc in range(32):
                    if oc + 1 < 32:
                        w = [mmh[(oc - 1, 1)]] if oc >= 1 else []
                        ld[oc + 1] = dma("gp", WS[(oc + 1) % 2][:, :],
                                         w_ff1[oc + 1], waits=w)
                    for sl in range(2):
                        tsl = half * 2 + sl
                        bk = 2 + (oc * 2 + sl) % 2
                        w = [ld[oc]]
                        if oc == 0 and sl == 0:
                            w.append(y1_done)
                            if prev_ff2_ev is not None:
                                w.append(prev_ff2_ev)
                        if evh[(oc * 2 + sl) % 2] is not None:
                            w.append(evh[(oc * 2 + sl) % 2])

                        def mk_mm(t, oc=oc, sl=sl, tsl=tsl, bk=bk):
                            wb = WS[oc % 2]
                            for ic in range(8):
                                t.matmul(pbank(bk)[:],
                                         wb[:, ic * 128:(ic + 1) * 128],
                                         A1[:, ic * T + tsl * 512: ic * T + (tsl + 1) * 512],
                                         start=(ic == 0), stop=(ic == 7))
                        mmh[(oc, sl)] = comp("pe", mk_mm, waits=w)

                        def mk_ev(a, oc=oc, sl=sl, bk=bk):
                            ar = harena[oc // 16]
                            oo = (oc % 16) * 1024 + sl * 512
                            a.activation(out=ar[:, oo:oo + 512], in_=pbank(bk)[:],
                                         func=AF.Gelu,
                                         bias=s_bff1[:, oc:oc + 1], scale=1.0)
                        evh[(oc * 2 + sl) % 2] = comp(
                            "act", mk_ev, waits=[mmh[(oc, sl)]])
                h_done = ("act", sch.now("act"))

                for sl in range(2):
                    tsl = half * 2 + sl
                    ld2 = {}
                    ld2[0] = dma("gp", WS[2][:, :], w_ff2[0:128, :],
                                 waits=[("pe", sch.now("pe"))])
                    mmlast = None
                    for oc in range(32):
                        if oc + 1 < 32:
                            w = [mmlast] if oc >= 1 else []
                            ld2[oc + 1] = dma("gp", WS[2 + (oc + 1) % 2][:, :],
                                              w_ff2[(oc + 1) * 128:(oc + 2) * 128, :],
                                              waits=w)
                        w = [ld2[oc]]
                        if oc == 0:
                            w.append(h_done)
                            if prev_ff2_ev is not None:
                                w.append(prev_ff2_ev)

                        def mk_mm(t, oc=oc, sl=sl):
                            wb = WS[2 + oc % 2]
                            ar = harena[oc // 16]
                            oo = (oc % 16) * 1024 + sl * 512
                            for f2 in range(8):
                                t.matmul(pbank(f2)[:],
                                         wb[:, f2 * 128:(f2 + 1) * 128],
                                         ar[:, oo:oo + 512],
                                         start=(oc == 0), stop=(oc == 31))
                        mmlast = comp("pe", mk_mm, waits=w)

                    def mk_ev(v, tsl=tsl):
                        for f2 in range(8):
                            v.scalar_tensor_tensor(
                                out=A2[:, f2 * T + tsl * 512: f2 * T + (tsl + 1) * 512],
                                in0=pbank(f2)[:], scalar=s_pc[:, 56 + f2:57 + f2],
                                in1=A1[:, f2 * T + tsl * 512: f2 * T + (tsl + 1) * 512],
                                op0=ALU.add, op1=ALU.add)
                    prev_ff2_ev = comp("dve", mk_ev, waits=[mmlast])

            fm_to_tm(A2, A3, waits=[("dve", sch.now("dve")), ("act", sch.now("act"))])
            l2 = ln_tm(A3, A3, waits=[("dve", sch.now("dve"))])
            tm_to_fm(A3, A2, waits=[l2],
                     gcol=lambda fc: s_pc[:, 24 + fc:25 + fc],
                     bcol=lambda fc: s_pc[:, 32 + fc:33 + fc])
            fm_to_tm(A2, A3, waits=[("dve", sch.now("dve"))])
            l3 = ln_tm(A3, A3, waits=[("dve", sch.now("dve"))])
            xld2 = []
            for ti in range(NTILE):
                xld2.append(dma("gp", A2[:, ti * E:(ti + 1) * E],
                                xin[b, ti * 128:(ti + 1) * 128, :],
                                waits=([l3] if ti == 0 else [])))

            def mk_x1(v):
                for ti in range(NTILE):
                    v.tensor_tensor(out=A1[:, ti * E:(ti + 1) * E],
                                    in0=A2[:, ti * E:(ti + 1) * E],
                                    in1=A3[:, ti * E:(ti + 1) * E], op=ALU.add)
            x1t = comp("dve", mk_x1, waits=[l3, ("gd", sch.now("gd"))])

            if stages <= 4:
                for ti in range(NTILE):
                    w = [x1t] if ti == 0 else [("sd", sch.now("sd"))]
                    def mk_c(v, ti=ti):
                        v.tensor_copy(out=t32a[:, 0:E], in_=A1[:, ti * E:(ti + 1) * E])
                    ftg = comp("dve", mk_c, waits=w)
                    dma("sync", xout[b, ti * 128:(ti + 1) * 128, :], t32a[:, 0:E], waits=[ftg])
                continue

            lz = ln_tm(A1, A2, waits=[x1t])
            tm_to_fm(A2, A3, waits=[lz])
            zfm_done = ("dve", sch.now("dve"))

            def mk_mix2(v):
                for fc in range(FC):
                    zf = A3[:, fc * T:(fc + 1) * T]
                    xr = A2[:, fc * T:(fc + 1) * T]
                    xk = AO[:, fc * T:(fc + 1) * T]
                    mur = s_pc[:, 40 + fc:41 + fc]
                    muk = s_pc[:, 48 + fc:49 + fc]
                    v.tensor_scalar(out=xr[:, 0:1], in0=zf[:, 0:1], scalar1=mur,
                                    scalar2=None, op0=ALU.mult)
                    v.tensor_scalar(out=xk[:, 0:1], in0=zf[:, 0:1], scalar1=muk,
                                    scalar2=None, op0=ALU.mult)
                    for cchunk in range(2):
                        lo = 1 + cchunk * 1023
                        hi = lo + 1023 + (1 if cchunk == 1 else 0)
                        n = hi - lo
                        v.tensor_tensor(out=t32a[:, 0:n], in0=zf[:, lo:lo + n],
                                        in1=zf[:, lo - 1:lo - 1 + n], op=ALU.subtract)
                        v.scalar_tensor_tensor(
                            out=xr[:, lo:lo + n], in0=t32a[:, 0:n], scalar=mur,
                            in1=zf[:, lo - 1:lo - 1 + n], op0=ALU.mult, op1=ALU.add)
                        v.scalar_tensor_tensor(
                            out=xk[:, lo:lo + n], in0=t32a[:, 0:n], scalar=muk,
                            in1=zf[:, lo - 1:lo - 1 + n], op0=ALU.mult, op1=ALU.add)
            mixdone = comp("dve", mk_mix2, waits=[zfm_done])

            def kact_ev(v, bank, oc, tsl):
                v.tensor_scalar(out=t32a[:, 0:512], in0=bank[:], scalar1=0.0,
                                scalar2=None, op0=ALU.max)
                v.drain()
                v.tensor_tensor(
                    out=A3[:, oc * T + tsl * 512: oc * T + (tsl + 1) * 512],
                    in0=t32a[:, 0:512], in1=t32a[:, 0:512], op=ALU.mult)
            proj_fm(w_k, 8, 8, AO, kact_ev, waits=[mixdone], wbufs=(0, 1),
                    pbanks=(2, 3))
            kact_done = ("dve", sch.now("dve"))

            ldr = {}
            ldv = {}
            ldr[0] = dma("gp", WS[0][:, :], w_r[0], waits=[kact_done])
            ldv[0] = dma("gp", WS[2][:, :], w_v2[0], waits=[kact_done])
            evr = [None, None]
            sig = [None, None]
            mmr = {}
            for oc in range(8):
                if oc + 1 < 8:
                    w = [mmr[(oc - 1, 3, "v")]] if oc >= 1 else []
                    ldr[oc + 1] = dma("gp", WS[(oc + 1) % 2][:, :], w_r[oc + 1], waits=w)
                    ldv[oc + 1] = dma("gp", WS[2 + (oc + 1) % 2][:, :], w_v2[oc + 1], waits=w)
                for tsl in range(4):
                    i2 = (oc * 4 + tsl) % 2
                    bkr = 4 + i2
                    bkv = 6 + i2
                    w = [ldr[oc], ldv[oc]]
                    if evr[i2] is not None:
                        w.append(evr[i2])

                    def mk_mm(t, oc=oc, tsl=tsl, bkr=bkr, bkv=bkv):
                        for ic in range(8):
                            t.matmul(pbank(bkr)[:],
                                     WS[oc % 2][:, ic * 128:(ic + 1) * 128],
                                     A2[:, ic * T + tsl * 512: ic * T + (tsl + 1) * 512],
                                     start=(ic == 0), stop=(ic == 7))
                        for ic in range(8):
                            t.matmul(pbank(bkv)[:],
                                     WS[2 + oc % 2][:, ic * 128:(ic + 1) * 128],
                                     A3[:, ic * T + tsl * 512: ic * T + (tsl + 1) * 512],
                                     start=(ic == 0), stop=(ic == 7))
                    mmr[(oc, tsl, "v")] = comp("pe", mk_mm, waits=w)

                    def mk_sig(a, oc=oc, tsl=tsl, bkr=bkr, i2=i2):
                        a.activation(out=t32a[:, i2 * 512:i2 * 512 + 512],
                                     in_=pbank(bkr)[:], func=AF.Sigmoid)
                    sig[i2] = comp("act", mk_sig, waits=[mmr[(oc, tsl, "v")]])

                    def mk_out(v, oc=oc, tsl=tsl, bkv=bkv, i2=i2):
                        v.tensor_tensor(
                            out=AO[:, oc * T + tsl * 512: oc * T + (tsl + 1) * 512],
                            in0=t32a[:, i2 * 512:i2 * 512 + 512],
                            in1=pbank(bkv)[:], op=ALU.mult)
                    evr[i2] = comp("dve", mk_out, waits=[sig[i2]])

            fm_to_tm(AO, A2, waits=[("dve", sch.now("dve")), ("act", sch.now("act"))])
            lr = ln_tm(A2, A2, waits=[("dve", sch.now("dve"))])

            o32b = A3[:].bitcast(F32)
            sttile = {}
            for ti in range(NTILE):
                w = [lr] if ti == 0 else []
                if ti >= 2:
                    w.append(sttile[ti - 2])
                off = (ti % 2) * E

                def mk_fin(v, ti=ti, off=off):
                    v.tensor_tensor(out=o32b[:, off:off + E],
                                    in0=A1[:, ti * E:(ti + 1) * E],
                                    in1=A2[:, ti * E:(ti + 1) * E], op=ALU.add)
                ftg = comp("dve", mk_fin, waits=w)
                sttile[ti] = dma("sync", xout[b, ti * 128:(ti + 1) * 128, :],
                                 o32b[:, off:off + E], waits=[ftg])

        def run_engine(ename):
            def go(e):
                for w, fn, tgt in sch.ops[ename]:
                    fn(e, w)
            return go

        block.tensor(run_engine("pe"))
        block.vector(run_engine("dve"))
        block.scalar(run_engine("act"))
        block.sync(run_engine("sync"))
        block.gpsimd(run_engine("gp"))

        def fin(e):
            e.wait_ge(sems["sd"], sch.c["sd"])
            e.wait_ge(sems["gd"], sch.c["gd"])
        block.sync(fin)

    return nc


def _pack_oc(wT, n_oc):
    K, M = wT.shape
    nic = K // 128
    w4 = wT.reshape(nic, 128, n_oc, 128)
    return np.ascontiguousarray(w4.transpose(2, 1, 0, 3).reshape(n_oc, 128, K))


def _aux_inputs():
    aux = {}
    i64 = np.arange(64, dtype=np.float32)
    i128 = np.arange(128, dtype=np.float32)
    i2048 = np.arange(2048, dtype=np.float32)
    aux["idn_bf"] = np.eye(128, dtype=np.float32)
    aux["idn_f16"] = np.eye(128, dtype=np.float32)
    aux["supper_bf"] = (i128[:, None] < i128[None, :]).astype(np.float32)
    v160 = np.arange(160, dtype=np.float32)
    aux["slt_a"] = (i128[:, None] < v160[None, :]).astype(np.float32)
    aux["slt_b"] = ((i128[:32, None] + 128) < v160[None, :]).astype(np.float32)
    aux["iota64r"] = np.broadcast_to(i64, (128, 64)).copy()
    aux["iota128r"] = np.broadcast_to(i128, (128, 128)).copy()
    aux["iota2048r"] = np.broadcast_to(i2048, (128, 2048)).copy()
    aux["iotacol"] = i128[:, None].copy()
    aux["thresh64"] = np.broadcast_to(128.0 * i64, (128, 64)).copy()
    aux["ones_f32"] = np.ones((128, 128), np.float32)
    aux["ones_bf"] = np.ones((128, 128), np.float32)
    aux["ones_f16"] = np.ones((128, 128), np.float32)
    d = (np.arange(256)[None, :] - 128 - np.arange(128)[:, None])
    m = np.where(d > 0, MVAL, np.where(d == 0, TSV, 0.0)).astype(np.float32)
    aux["mask_w"] = m
    m0 = m.copy()
    m0[:, :128] = MVAL
    aux["mask_w0"] = m0
    cc = np.zeros((128, 4), np.float32)
    cc[:, 0] = LNE; cc[:, 1] = EPSN; cc[:, 2] = 1.0
    aux["constc"] = cc
    return aux


_BUILT = {}


def kernel(L0_attn_x, L0_attn_w_qk, L0_attn_w_v, L0_attn_w_out, L0_attn_b_out,
           L0_attn_w_ff1, L0_attn_b_ff1, L0_attn_w_ff2, L0_attn_b_ff2,
           L0_attn_ln1_g, L0_attn_ln1_b, L0_attn_ln2_g, L0_attn_ln2_b,
           L0_attn_rotations, L0_ffn_W_r, L0_ffn_W_k, L0_ffn_W_v,
           L0_ffn_mu_r, L0_ffn_mu_k):
    import ml_dtypes
    f32 = np.float32

    def packcols(vecs):
        out = np.zeros((128, 64), f32)
        for j, v in enumerate(vecs):
            out[:, j * 8:(j + 1) * 8] = np.asarray(v, f32).reshape(8, 128).T
        return out

    pc = np.zeros((128, 64), f32)
    for j, v in enumerate([L0_attn_b_out, L0_attn_ln1_g, L0_attn_ln1_b,
                           L0_attn_ln2_g, L0_attn_ln2_b, L0_ffn_mu_r,
                           L0_ffn_mu_k, L0_attn_b_ff2]):
        pc[:, j * 8:(j + 1) * 8] = np.asarray(v, f32).reshape(8, 128).T
    bff1c = np.asarray(L0_attn_b_ff1, f32).reshape(32, 128).T.copy()

    common = {
        "w_qk": _pack_oc(np.ascontiguousarray(np.asarray(L0_attn_w_qk, f32).T), 8),
        "w_v": _pack_oc(np.ascontiguousarray(np.asarray(L0_attn_w_v, f32).T), 8),
        "w_out": _pack_oc(np.ascontiguousarray(np.asarray(L0_attn_w_out, f32).T), 8),
        "w_ff1": _pack_oc(np.ascontiguousarray(np.asarray(L0_attn_w_ff1, f32).T), 32),
        "w_ff2": np.ascontiguousarray(np.asarray(L0_attn_w_ff2, f32).T),
        "w_r": _pack_oc(np.ascontiguousarray(np.asarray(L0_ffn_W_r, f32).T), 8),
        "w_k": _pack_oc(np.ascontiguousarray(np.asarray(L0_ffn_W_k, f32).T), 8),
        "w_v2": _pack_oc(np.ascontiguousarray(np.asarray(L0_ffn_W_v, f32).T), 8),
        "rotc": np.ascontiguousarray(np.asarray(L0_attn_rotations, f32).reshape(DH, 128)),
        "pcols": pc, "bff1c": bff1c,
    }
    aux = _aux_inputs()
    for k in ("idn_bf", "supper_bf", "ones_bf"):
        aux[k] = aux[k].astype(ml_dtypes.bfloat16)
    for k in ("idn_f16", "ones_f16", "iota2048r"):
        aux[k] = aux[k].astype(np.float16)
    common.update(aux)

    import os
    stages = int(os.environ.get("KSTAGES", "99"))
    key = ("nc", stages)
    if key not in _BUILT:
        _BUILT[key] = build_nc(stages)
    nc = _BUILT[key]

    x = np.asarray(L0_attn_x, f32)
    ncores = int(os.environ.get("KCORES", "8"))
    in_maps = []
    for core in range(ncores):
        m = dict(common)
        m["xin"] = np.ascontiguousarray(x[2 * core:2 * core + 2])
        in_maps.append(m)
    res = run_bass_kernel_spmd(nc, in_maps, core_ids=list(range(ncores)))
    out = np.concatenate([np.asarray(r["xout"]) for r in res.results], axis=0)
    if ncores < 8:
        out = np.concatenate([out] + [np.zeros_like(out[:2])] * (8 - ncores), axis=0)
    return out.reshape(16, T, E).astype(np.float32)
